# version 05 (final)
# baseline (speedup 1.0000x reference)
import contextlib
import numpy as np
import ml_dtypes
import concourse.bass as bass
import concourse.mybir as mybir
from concourse.bass_utils import run_bass_kernel_spmd

F32 = mybir.dt.float32
BF16 = mybir.dt.bfloat16
I32 = mybir.dt.int32
AF = mybir.ActivationFunctionType
ALU = mybir.AluOpType
AX = mybir.AxisListType
BF = ml_dtypes.bfloat16

T = 2048
E = 2048
II = 5632
H = 16
D = 128
QR = 64
NFC = E // 128
NIC = II // 128
TB = 512
NTB = T // TB
EPS = 1e-5
SM_SCALE = 1.0 / float(np.sqrt(128.0))
MBIAS = -20.0
INV127 = 1.0 / 127.0
INV_E = 1.0 / float(E)


class _Ctx:
    pass


def _mk_sems(c, nc, stack):
    c.sem = {}
    c.n = {}
    for k in ("ld", "gp", "pe", "ve", "se"):
        c.sem[k] = stack.enter_context(nc.semaphore(k))
        c.n[k] = 0


def _w(eng, c, key, val):
    if val > 0:
        eng.wait_ge(c.sem[key], val)


def _inc(c, key, inst, k=1):
    inst.then_inc(c.sem[key], k)
    c.n[key] += k
    return c.n[key]


def _join(c):
    snap = dict(c.n)
    blk = c.block

    @blk.sync
    def _(e):
        for k in ("gp", "pe", "ve", "se"):
            _w(e, c, k, snap[k])

    @blk.gpsimd
    def _(e):
        for k in ("ld", "pe", "ve", "se"):
            _w(e, c, k, snap[k])

    @blk.tensor
    def _(e):
        for k in ("ld", "gp", "ve", "se"):
            _w(e, c, k, snap[k])

    @blk.scalar
    def _(e):
        for k in ("ld", "gp", "pe", "ve"):
            _w(e, c, k, snap[k])

    @blk.vector
    def _(e):
        for k in ("ld", "gp", "pe", "se"):
            _w(e, c, k, snap[k])


def _emit_scale_dance(c, amb, W, bufs, tag):
    nw = W // 128
    (ps_tr, cbf, ps_tr2, sgrid, s_row, rs_row) = bufs
    blk = c.block

    for g in range((nw + 3) // 4):
        j0, j1 = g * 4, min(nw, g * 4 + 4)

        @blk.tensor
        def _(te, j0=j0, j1=j1):
            _w(te, c, "ve", c.n["ve"])
            for j in range(j0, j1):
                inst = te.transpose(ps_tr[:, (j - j0) * 128:(j - j0 + 1) * 128], amb[:, j * 128:(j + 1) * 128], c.identb[:])
            _inc(c, "pe", inst)

        pe_tr = c.n["pe"]

        @blk.vector
        def _(ve, j0=j0, j1=j1, pe_tr=pe_tr):
            _w(ve, c, "pe", pe_tr)
            for j in range(j0, j1):
                inst = ve.reduce_max(cbf[:, j:j + 1], ps_tr[:, (j - j0) * 128:(j - j0 + 1) * 128], AX.X)
            _inc(c, "ve", inst)

    ve_c = c.n["ve"]

    @blk.tensor
    def _(te):
        _w(te, c, "ve", ve_c)
        _inc(c, "pe", te.transpose(ps_tr2[0:nw, 0:128], cbf[:, 0:nw], c.identb[:]))

    pe_t2 = c.n["pe"]

    @blk.scalar
    def _(se):
        _w(se, c, "pe", pe_t2)
        _inc(c, "se", se.copy(sgrid[0:nw, :], ps_tr2[0:nw, 0:128]))

    se_g = c.n["se"]

    @blk.gpsimd
    def _(gp):
        _w(gp, c, "se", se_g)
        _inc(c, "gp", gp.dma_start(out=s_row[:, 0:W], in_=sgrid[0:nw, :]), 16)

    gp_r = c.n["gp"]

    @blk.scalar
    def _(se):
        _w(se, c, "gp", gp_r)
        se.activation(rs_row[:, 0:W], s_row[:, 0:W], AF.Ln, bias=127e-8)
        _inc(c, "se", se.activation(rs_row[:, 0:W], rs_row[:, 0:W], AF.Exp, scale=-1.0))


def _emit_bcast(c, row_ap, dst, ps_bc, lhs=None, evac_eng="scalar", extra=None):
    blk = c.block
    lhs = lhs if lhs is not None else c.ones1
    se_pre = c.n["se"]

    @blk.tensor
    def _(te):
        _w(te, c, "se", se_pre)
        _w(te, c, "ve", c.n["ve"])
        if extra is not None:
            _w(te, c, extra[0], extra[1])
        _inc(c, "pe", te.matmul(ps_bc[:], lhs[:], row_ap, start=True, stop=True))

    pe_bc = c.n["pe"]
    if evac_eng == "scalar":
        @blk.scalar
        def _(se):
            _w(se, c, "pe", pe_bc)
            _inc(c, "se", se.copy(dst[:], ps_bc[:]))
    else:
        @blk.vector
        def _(ve):
            _w(ve, c, "pe", pe_bc)
            _inc(c, "ve", ve.tensor_copy(dst[:], ps_bc[:]))


def _emit_ffn(c, nc, x_in, x_out, phase):
    blk = c.block
    with contextlib.ExitStack() as st:
        en = st.enter_context
        c.hs_row = en(nc.sbuf_tensor(f"f_hsrow_{phase}", [1, T], F32))
        c.hrs_row = en(nc.sbuf_tensor(f"f_hrsrow_{phase}", [1, T], F32))
        stAB = st.enter_context(contextlib.ExitStack())
        xq_all = stAB.enter_context(nc.sbuf_tensor(f"xq{phase}_{phase}", [128, NFC * T], BF16))

        with contextlib.ExitStack() as stA:
            ea = stA.enter_context
            Xb = ea(nc.sbuf_tensor(f"pa_xb_{phase}", [128, NFC * TB], F32))
            xn = ea(nc.sbuf_tensor(f"pa_xn_{phase}", [128, NFC * TB], BF16))
            tmp_b = [ea(nc.sbuf_tensor(f"pa_tmpb_{phase}_{i}", [128, TB], BF16)) for i in range(2)]
            tmp_b2 = [ea(nc.sbuf_tensor(f"pa_tmpb2_{phase}_{i}", [128, TB], BF16)) for i in range(2)]
            t1 = ea(nc.sbuf_tensor(f"pa_t1_{phase}", [128, TB], BF16))
            ci = ea(nc.sbuf_tensor(f"pa_ci_{phase}", [128, TB], I32))
            cf = ea(nc.sbuf_tensor(f"pa_cf_{phase}", [128, TB], BF16))
            dd = ea(nc.sbuf_tensor(f"pa_dd_{phase}", [128, TB], BF16))
            qq = ea(nc.sbuf_tensor(f"pa_qq_{phase}", [128, TB], BF16))
            amb = ea(nc.sbuf_tensor(f"pa_amb_{phase}", [128, TB], BF16))
            rows = ea(nc.sbuf_tensor(f"pa_rows_{phase}", [1, 8 * TB], F32))
            cbf = ea(nc.sbuf_tensor(f"pa_cbf_{phase}", [128, 4], BF16))
            sgrid = ea(nc.sbuf_tensor(f"pa_sgrid_{phase}", [4, 128], F32))
            mu_b = ea(nc.sbuf_tensor(f"pa_mub_{phase}", [128, TB], F32))
            r_b = ea(nc.sbuf_tensor(f"pa_rb_{phase}", [128, TB], F32))
            rs_b = ea(nc.sbuf_tensor(f"pa_rsb_{phase}", [128, TB], BF16))
            s_b = ea(nc.sbuf_tensor(f"pa_sb_{phase}", [128, TB], BF16))
            ps1 = ea(nc.psum_tensor(f"pa_ps1_{phase}", [128, TB], F32))
            ps2 = ea(nc.psum_tensor(f"pa_ps2_{phase}", [128, TB], F32))
            ps_bc = ea(nc.psum_tensor(f"pa_psbc_{phase}", [128, TB], F32))
            ps_tr = ea(nc.psum_tensor(f"pa_pstr_{phase}", [128, TB], BF16))
            ps_tr2 = ea(nc.psum_tensor(f"pa_pstr2_{phase}", [128, TB], BF16))

            SX, SX2, MU, RRW, VAR, SRW, RSRW = (i * TB for i in range(7))

            for tb in range(NTB):

                @blk.sync
                def _(sy):
                    _w(sy, c, "ve", c.n["ve"])
                    _w(sy, c, "se", c.n["se"])
                    src = x_in[:, tb * TB:(tb + 1) * TB].rearrange("(c p) t -> p c t", p=128)
                    _inc(c, "ld", sy.dma_start(out=Xb[:].rearrange("p (c t) -> p c t", c=NFC), in_=src), 16)

                ld_x = c.n["ld"]

                pe_fc = {}
                for fc in range(NFC):
                    s = fc % 2

                    @blk.scalar
                    def _(se, fc=fc, s=s):
                        if fc == 0:
                            _w(se, c, "ld", ld_x)
                            _w(se, c, "pe", c.n["pe"])
                        if fc >= 2:
                            _w(se, c, "pe", pe_fc[fc - 2])
                        se.copy(tmp_b[s][:], Xb[:, fc * TB:(fc + 1) * TB])
                        _inc(c, "se", se.activation(tmp_b2[s][:], Xb[:, fc * TB:(fc + 1) * TB], AF.Square))

                    se_now = c.n["se"]

                    @blk.tensor
                    def _(te, fc=fc, s=s, se_now=se_now):
                        _w(te, c, "se", se_now)
                        te.matmul(ps1[0:1, :], c.ones128b[:], tmp_b[s][:], start=(fc == 0), stop=(fc == NFC - 1))
                        inst = te.matmul(ps2[0:1, :], c.ones128b[:], tmp_b2[s][:], start=(fc == 0), stop=(fc == NFC - 1))
                        _inc(c, "pe", inst)

                    pe_fc[fc] = c.n["pe"]

                pe_stats = c.n["pe"]

                @blk.scalar
                def _(se):
                    _w(se, c, "pe", pe_stats)
                    se.copy(rows[:, SX:SX + TB], ps1[0:1, :])
                    _inc(c, "se", se.copy(rows[:, SX2:SX2 + TB], ps2[0:1, :]))

                se_rows = c.n["se"]

                @blk.vector
                def _(ve):
                    _w(ve, c, "se", se_rows)
                    ve.tensor_scalar(rows[:, MU:MU + TB], rows[:, SX:SX + TB], INV_E, None, ALU.mult)
                    ve.tensor_scalar(rows[:, VAR:VAR + TB], rows[:, SX2:SX2 + TB], INV_E, None, ALU.mult)
                    ve.tensor_tensor(rows[:, SRW:SRW + TB], rows[:, MU:MU + TB], rows[:, MU:MU + TB], ALU.mult)
                    _inc(c, "ve", ve.tensor_tensor(rows[:, VAR:VAR + TB], rows[:, VAR:VAR + TB], rows[:, SRW:SRW + TB], ALU.subtract))

                ve_var = c.n["ve"]

                @blk.scalar
                def _(se):
                    _w(se, c, "ve", ve_var)
                    se.activation(rows[:, RRW:RRW + TB], rows[:, VAR:VAR + TB], AF.Ln, bias=EPS)
                    _inc(c, "se", se.activation(rows[:, RRW:RRW + TB], rows[:, RRW:RRW + TB], AF.Exp, scale=-0.5))

                _emit_bcast(c, rows[0:1, MU:MU + TB], mu_b, ps_bc, evac_eng="scalar")
                _emit_bcast(c, rows[0:1, RRW:RRW + TB], r_b, ps_bc, evac_eng="scalar")
                se_bc = c.n["se"]

                @blk.vector
                def _(ve):
                    _w(ve, c, "se", se_bc)
                    for fc in range(NFC):
                        ve.tensor_tensor(Xb[:, fc * TB:(fc + 1) * TB], Xb[:, fc * TB:(fc + 1) * TB], mu_b[:], ALU.subtract)
                        ve.tensor_tensor(xn[:, fc * TB:(fc + 1) * TB], Xb[:, fc * TB:(fc + 1) * TB], r_b[:], ALU.mult)
                    _inc(c, "ve", ve.reduce_max(amb[:], xn[:].rearrange("p (c t) -> p t c", c=NFC), AX.X, apply_absolute_value=True))

                _emit_scale_dance(c, amb, TB, (ps_tr, cbf, ps_tr2, sgrid,
                                               rows[:, SRW:SRW + TB], rows[:, RSRW:RSRW + TB]), f"pa{phase}_{tb}")
                _emit_bcast(c, rows[0:1, RSRW:RSRW + TB], rs_b, ps_bc, lhs=c.ones127, evac_eng="scalar")
                _emit_bcast(c, rows[0:1, SRW:SRW + TB], s_b, ps_bc, lhs=c.onesinv, evac_eng="scalar")
                se_q = c.n["se"]

                for fc in range(NFC):
                    @blk.vector
                    def _(ve, fc=fc):
                        if fc == 0:
                            _w(ve, c, "se", se_q)
                        _inc(c, "ve", ve.tensor_tensor(t1[:], xn[:, fc * TB:(fc + 1) * TB], rs_b[:], ALU.mult))

                    ve_t1 = c.n["ve"]

                    @blk.scalar
                    def _(se, ve_t1=ve_t1):
                        _w(se, c, "ve", ve_t1)
                        se.activation(ci[:], t1[:], AF.Copy, bias=0.5)
                        _inc(c, "se", se.copy(cf[:], ci[:]))

                    se_cf = c.n["se"]

                    @blk.vector
                    def _(ve, fc=fc, se_cf=se_cf):
                        _w(ve, c, "se", se_cf)
                        ve.scalar_tensor_tensor(dd[:], cf[:], -0.5, t1[:], ALU.add, ALU.is_gt)
                        ve.tensor_tensor(qq[:], cf[:], dd[:], ALU.subtract)
                        _inc(c, "ve", ve.tensor_tensor(xq_all[:, fc * T + tb * TB: fc * T + (tb + 1) * TB], qq[:], s_b[:], ALU.mult))

                if tb == 0 and getattr(c, "dbg", None) is not None and phase == 0:
                    _join(c)

                    @blk.gpsimd
                    def _(gp):
                        for nm, src in (("rows0", rows), ("rsb0", rs_b), ("sb0", s_b),
                                        ("amb0", amb), ("cbf0", cbf),
                                        ("sgrid0", sgrid)):
                            _inc(c, "gp", gp.dma_start(out=c.dbg[nm], in_=src[:]), 16)
                    _join(c)
            _join(c)
            if getattr(c, "dbg", None) is not None and phase == 0:
                @blk.gpsimd
                def _(gp):
                    for nm, src in (("rows", rows), ("mub", mu_b), ("rb", r_b),
                                    ("rsb", rs_b), ("sb", s_b), ("xn", xn), ("amb", amb)):
                        _inc(c, "gp", gp.dma_start(out=c.dbg[nm], in_=src[:]), 16)
                _join(c)

        if getattr(c, "dbg", None) is not None and phase == 0:
            @blk.gpsimd
            def _(gp):
                _inc(c, "gp", gp.dma_start(out=c.dbg["xq"], in_=xq_all[:]), 16)
            _join(c)

        with contextlib.ExitStack() as stB:
            eb = stB.enter_context
            NW1 = 2
            w1 = [eb(nc.sbuf_tensor(f"pb_w1_{i}_{phase}", [128, NFC * 256], BF16)) for i in range(NW1)]
            silu_g = [eb(nc.sbuf_tensor(f"pb_silu_{phase}_{i}", [128, TB], BF16)) for i in range(2)]
            hbuf = [eb(nc.sbuf_tensor(f"pb_h_{i}_{phase}", [128, TB], BF16)) for i in range(3)]
            am_mx = eb(nc.sbuf_tensor(f"pb_amx_{phase}", [128, T], F32))
            am_mn = eb(nc.sbuf_tensor(f"pb_amn_{phase}", [128, T], F32))
            ps_g = [eb(nc.psum_tensor(f"pb_psg_{i}_{phase}", [128, TB], F32)) for i in range(2)]
            ps_u = [eb(nc.psum_tensor(f"pb_psu_{i}_{phase}", [128, TB], F32)) for i in range(2)]

            @blk.vector
            def _(ve):
                ve.memset(am_mx[:], 0.0)
                _inc(c, "ve", ve.memset(am_mn[:], 0.0))

            ve0 = c.n["ve"]

            h_evts = {}
            w1_evts = [0] * NW1
            it = 0
            for j in range(NIC):
                wslot = j % NW1

                @blk.gpsimd
                def _(gp, j=j, wslot=wslot):
                    _w(gp, c, "pe", w1_evts[wslot])
                    src = c.w1T_f32[:, j * 256:(j + 1) * 256].rearrange("(c p) n -> p c n", p=128)
                    _inc(c, "gp", gp.dma_start(out=w1[wslot][:].rearrange("p (c n) -> p c n", c=NFC), in_=src), 16)

                gp_w1 = c.n["gp"]

                for tbb in range(NTB):
                    pslot = it % 2
                    ve_free = h_evts.get((j - 1, tbb), 0) if False else 0

                    @blk.tensor
                    def _(te, j=j, tbb=tbb, pslot=pslot, gp_w1=gp_w1, wslot=wslot):
                        _w(te, c, "gp", gp_w1)
                        _w(te, c, "ve", c.psfree.get(("pb", pslot), 0))
                        _w(te, c, "se", c.psfree_s.get(("pb", pslot), 0))
                        for fc in range(NFC):
                            te.matmul(ps_g[pslot][:], w1[wslot][:, fc * 256:fc * 256 + 128],
                                      xq_all[:, fc * T + tbb * TB: fc * T + (tbb + 1) * TB],
                                      start=(fc == 0), stop=(fc == NFC - 1))
                        for fc in range(NFC):
                            inst = te.matmul(ps_u[pslot][:], w1[wslot][:, fc * 256 + 128:fc * 256 + 256],
                                             xq_all[:, fc * T + tbb * TB: fc * T + (tbb + 1) * TB],
                                             start=(fc == 0), stop=(fc == NFC - 1))
                        _inc(c, "pe", inst)

                    pe_mm = c.n["pe"]
                    if tbb == NTB - 1:
                        w1_evts[wslot] = pe_mm

                    @blk.scalar
                    def _(se, pslot=pslot, pe_mm=pe_mm):
                        _w(se, c, "pe", pe_mm)
                        _inc(c, "se", se.activation(silu_g[pslot][:], ps_g[pslot][:], AF.Silu))

                    se_silu = c.n["se"]
                    c.psfree_s[("pb", pslot)] = se_silu
                    hslot = it % 3

                    @blk.vector
                    def _(ve, j=j, tbb=tbb, pslot=pslot, hslot=hslot, se_silu=se_silu):
                        _w(ve, c, "se", se_silu)
                        _w(ve, c, "ld", c.hfree.get(hslot, 0))
                        ve.tensor_tensor(hbuf[hslot][:], silu_g[pslot][:], ps_u[pslot][:], ALU.mult)
                        ve.tensor_tensor(am_mx[:, tbb * TB:(tbb + 1) * TB],
                                         am_mx[:, tbb * TB:(tbb + 1) * TB], hbuf[hslot][:], ALU.max)
                        _inc(c, "ve", ve.tensor_tensor(am_mn[:, tbb * TB:(tbb + 1) * TB],
                                                       am_mn[:, tbb * TB:(tbb + 1) * TB], hbuf[hslot][:], ALU.min))

                    ve_h = c.n["ve"]
                    c.psfree[("pb", pslot)] = ve_h

                    @blk.sync
                    def _(sy, j=j, tbb=tbb, hslot=hslot, ve_h=ve_h):
                        _w(sy, c, "ve", ve_h)
                        _inc(c, "ld", sy.dma_start(out=c.h_scr[j * 128:(j + 1) * 128, tbb * TB:(tbb + 1) * TB], in_=hbuf[hslot][:]), 16)

                    c.hfree[hslot] = c.n["ld"]
                    it += 1
            _join(c)

            with contextlib.ExitStack() as std:
                ed = std.enter_context
                amb_t = ed(nc.sbuf_tensor(f"pbd_amb_{phase}", [128, T], BF16))
                cbf = ed(nc.sbuf_tensor(f"pbd_cbf_{phase}", [128, NTB * 4], BF16))
                sgrid = ed(nc.sbuf_tensor(f"pbd_sgrid_{phase}", [NTB * 4, 128], F32))
                ps_tr = ed(nc.psum_tensor(f"pbd_pstr_{phase}", [128, TB], BF16))
                ps_tr2 = ed(nc.psum_tensor(f"pbd_pstr2_{phase}", [128, TB], BF16))

                @blk.vector
                def _(ve):
                    _inc(c, "ve", ve.scalar_tensor_tensor(amb_t[:], am_mn[:], -1.0, am_mx[:], ALU.mult, ALU.max))

                _emit_scale_dance(c, amb_t, T, (ps_tr, cbf, ps_tr2, sgrid,
                                                c.hs_row[:, 0:T], c.hrs_row[:, 0:T]), f"pbd{phase}")
                _join(c)

            if getattr(c, "dbg", None) is not None and phase == 0:
                @blk.gpsimd
                def _(gp):
                    _inc(c, "gp", gp.dma_start(out=c.dbg["hsrows"][0:1, :], in_=c.hs_row[:]), 16)
                    _inc(c, "gp", gp.dma_start(out=c.dbg["hsrows"][1:2, :], in_=c.hrs_row[:]), 16)
                _join(c)

        stAB.close()

        with contextlib.ExitStack() as stC:
            ec = stC.enter_context
            TBC = 1024
            qh = ec(nc.sbuf_tensor(f"pc_qh_{phase}", [128, NIC * TBC], BF16))
            hload = [ec(nc.sbuf_tensor(f"pc_hl_{i}_{phase}", [128, TBC], BF16)) for i in range(2)]
            t1 = ec(nc.sbuf_tensor(f"pc_t1_{phase}", [128, TBC], BF16))
            ci = ec(nc.sbuf_tensor(f"pc_ci_{phase}", [128, TBC], I32))
            cf = ec(nc.sbuf_tensor(f"pc_cf_{phase}", [128, TBC], BF16))
            dd = ec(nc.sbuf_tensor(f"pc_dd_{phase}", [128, TBC], BF16))
            NW2 = 2
            w2 = [ec(nc.sbuf_tensor(f"pc_w2_{i}_{phase}", [128, NIC * 128], BF16)) for i in range(NW2)]
            rs_b = ec(nc.sbuf_tensor(f"pc_rsb_{phase}", [128, TBC], BF16))
            sh_b = ec(nc.sbuf_tensor(f"pc_shb_{phase}", [128, TBC], F32))
            xchunk = [ec(nc.sbuf_tensor(f"pc_xc_{i}_{phase}", [128, TBC], F32)) for i in range(2)]
            xout = [ec(nc.sbuf_tensor(f"pc_xo_{i}_{phase}", [128, TBC], F32)) for i in range(2)]
            ps_d = [ec(nc.psum_tensor(f"pc_psd_{i}_{phase}", [128, TB], F32)) for i in range(2)]
            ps_bc = ec(nc.psum_tensor(f"pc_psbc_{phase}", [128, TB], F32))

            w2_evts = [0] * NW2
            for tb2 in range(T // TBC):
                for half in range(TBC // TB):
                    col = tb2 * TBC + half * TB
                    _emit_bcast(c, c.hrs_row[0:1, col:col + TB], rs_b[:, half * TB:(half + 1) * TB], ps_bc, lhs=c.ones127, evac_eng="scalar")
                    _emit_bcast(c, c.hs_row[0:1, col:col + TB], sh_b[:, half * TB:(half + 1) * TB], ps_bc, lhs=c.oneshalfinv, evac_eng="scalar")
                se_bc = c.n["se"]

                for icc in range(NIC):
                    hs = icc % 2

                    @blk.sync
                    def _(sy, icc=icc, tb2=tb2, hs=hs):
                        _w(sy, c, "ve", c.hfree.get(("pc", hs), 0))
                        _inc(c, "ld", sy.dma_start(out=hload[hs][:], in_=c.h_scr[icc * 128:(icc + 1) * 128, tb2 * TBC:(tb2 + 1) * TBC]), 16)

                    ld_h = c.n["ld"]

                    @blk.vector
                    def _(ve, icc=icc, hs=hs, ld_h=ld_h, se_bc=se_bc):
                        _w(ve, c, "ld", ld_h)
                        _w(ve, c, "se", se_bc)
                        _inc(c, "ve", ve.tensor_tensor(t1[:], hload[hs][:], rs_b[:], ALU.mult))

                    ve_t1 = c.n["ve"]
                    c.hfree[("pc", hs)] = ve_t1

                    @blk.scalar
                    def _(se, ve_t1=ve_t1):
                        _w(se, c, "ve", ve_t1)
                        se.activation(ci[:], t1[:], AF.Copy, bias=0.5)
                        _inc(c, "se", se.copy(cf[:], ci[:]))

                    se_cf = c.n["se"]

                    @blk.vector
                    def _(ve, icc=icc, se_cf=se_cf):
                        _w(ve, c, "se", se_cf)
                        ve.scalar_tensor_tensor(dd[:], cf[:], -0.5, t1[:], ALU.add, ALU.is_gt)
                        _inc(c, "ve", ve.tensor_tensor(qh[:, icc * TBC:(icc + 1) * TBC], cf[:], dd[:], ALU.subtract))

                ve_qh = c.n["ve"]

                for eo in range(NFC):
                    ws = eo % NW2

                    @blk.gpsimd
                    def _(gp, eo=eo, ws=ws):
                        _w(gp, c, "pe", w2_evts[ws])
                        src = c.w2T_f32[:, eo * 128:(eo + 1) * 128].rearrange("(c p) n -> p c n", p=128)
                        _inc(c, "gp", gp.dma_start(out=w2[ws][:].rearrange("p (c n) -> p c n", c=NIC), in_=src), 16)

                    gp_w2 = c.n["gp"]

                    @blk.sync
                    def _(sy, eo=eo, tb2=tb2):
                        _w(sy, c, "ve", c.xcfree.get(eo % 2, 0))
                        _inc(c, "ld", sy.dma_start(out=xchunk[eo % 2][:], in_=x_in[eo * 128:(eo + 1) * 128, tb2 * TBC:(tb2 + 1) * TBC]), 16)

                    ld_xc = c.n["ld"]

                    for half in range(TBC // TB):
                        pslot = (eo * 2 + half) % 2

                        @blk.tensor
                        def _(te, eo=eo, half=half, pslot=pslot, ws=ws, gp_w2=gp_w2, ve_qh=ve_qh):
                            _w(te, c, "gp", gp_w2)
                            _w(te, c, "ve", max(ve_qh, c.psfree.get(("pc", pslot), 0)))
                            for icc in range(NIC):
                                inst = te.matmul(ps_d[pslot][:], w2[ws][:, icc * 128:(icc + 1) * 128],
                                                 qh[:, icc * TBC + half * TB: icc * TBC + (half + 1) * TB],
                                                 start=(icc == 0), stop=(icc == NIC - 1))
                            _inc(c, "pe", inst)

                        pe_d = c.n["pe"]
                        if half == 1:
                            w2_evts[ws] = pe_d

                        @blk.vector
                        def _(ve, eo=eo, half=half, pslot=pslot, pe_d=pe_d, ld_xc=ld_xc):
                            _w(ve, c, "pe", pe_d)
                            _w(ve, c, "ld", ld_xc)
                            ve.tensor_tensor(xout[eo % 2][:, half * TB:(half + 1) * TB], ps_d[pslot][:], sh_b[:, half * TB:(half + 1) * TB], ALU.mult)
                            _inc(c, "ve", ve.tensor_tensor(xout[eo % 2][:, half * TB:(half + 1) * TB],
                                                           xout[eo % 2][:, half * TB:(half + 1) * TB],
                                                           xchunk[eo % 2][:, half * TB:(half + 1) * TB], ALU.add))

                        c.psfree[("pc", pslot)] = c.n["ve"]

                    ve_xo = c.n["ve"]
                    c.xcfree[eo % 2] = ve_xo

                    @blk.sync
                    def _(sy, eo=eo, tb2=tb2, ve_xo=ve_xo):
                        _w(sy, c, "ve", ve_xo)
                        _inc(c, "ld", sy.dma_start(out=x_out[eo * 128:(eo + 1) * 128, tb2 * TBC:(tb2 + 1) * TBC], in_=xout[eo % 2][:]), 16)
            _join(c)


def _emit_attn(c, nc, x_io):
    blk = c.block
    with contextlib.ExitStack() as st:
        en = st.enter_context
        K_res = en(nc.sbuf_tensor("at_K", [128, H * T], BF16))

        with contextlib.ExitStack() as s1:
            e1 = s1.enter_context
            Xb = e1(nc.sbuf_tensor("p1_xb", [128, NFC * TB], F32))
            a1 = e1(nc.sbuf_tensor("p1_a1", [128, NFC * TB], BF16))
            aa = e1(nc.sbuf_tensor("p1_a", [128, NFC * TB], BF16))
            tmp_b = [e1(nc.sbuf_tensor(f"p1_tmpb_{i}", [128, TB], BF16)) for i in range(2)]
            tmp_b2 = [e1(nc.sbuf_tensor(f"p1_tmpb2_{i}", [128, TB], BF16)) for i in range(2)]
            rows = e1(nc.sbuf_tensor("p1_rows", [1, 8 * TB], F32))
            mu_b = e1(nc.sbuf_tensor("p1_mub", [128, TB], F32))
            r_b = e1(nc.sbuf_tensor("p1_rb", [128, TB], F32))
            aq1 = e1(nc.sbuf_tensor("p1_aq1", [64, TB], BF16))
            kc_sb = e1(nc.sbuf_tensor("p1_kc", [128, TB], BF16))
            qk_sb = [e1(nc.sbuf_tensor(f"p1_qksb_{i}", [128, TB], BF16)) for i in range(2)]
            tcos = e1(nc.sbuf_tensor("p1_tcos", [64, TB], BF16))
            tsin = e1(nc.sbuf_tensor("p1_tsin", [64, TB], BF16))
            qfin = [e1(nc.sbuf_tensor(f"p1_qfin_{i}", [128, TB], BF16)) for i in range(2)]
            ps1 = e1(nc.psum_tensor("p1_ps1", [128, TB], F32))
            ps2 = e1(nc.psum_tensor("p1_ps2", [128, TB], F32))
            ps_bc = e1(nc.psum_tensor("p1_psbc", [128, TB], F32))
            ps_p = [e1(nc.psum_tensor(f"p1_psp_{i}", [128, TB], F32)) for i in range(2)]
            ps_B = e1(nc.psum_tensor("p1_psB", [128, TB], F32))

            SX, SX2, MU, RRW, VAR = (i * TB for i in range(5))

            for tb in range(NTB):
                @blk.sync
                def _(sy, tb=tb):
                    _w(sy, c, "ve", c.n["ve"])
                    _w(sy, c, "se", c.n["se"])
                    src = x_io[:, tb * TB:(tb + 1) * TB].rearrange("(c p) t -> p c t", p=128)
                    _inc(c, "ld", sy.dma_start(out=Xb[:].rearrange("p (c t) -> p c t", c=NFC), in_=src), 16)

                ld_x = c.n["ld"]

                pe_fc = {}
                for fc in range(NFC):
                    s = fc % 2

                    @blk.scalar
                    def _(se, fc=fc, s=s):
                        if fc == 0:
                            _w(se, c, "ld", ld_x)
                            _w(se, c, "pe", c.n["pe"])
                        if fc >= 2:
                            _w(se, c, "pe", pe_fc[fc - 2])
                        se.copy(tmp_b[s][:], Xb[:, fc * TB:(fc + 1) * TB])
                        _inc(c, "se", se.activation(tmp_b2[s][:], Xb[:, fc * TB:(fc + 1) * TB], AF.Square))

                    se_now = c.n["se"]

                    @blk.tensor
                    def _(te, fc=fc, s=s, se_now=se_now):
                        _w(te, c, "se", se_now)
                        te.matmul(ps1[0:1, :], c.ones128b[:], tmp_b[s][:], start=(fc == 0), stop=(fc == NFC - 1))
                        inst = te.matmul(ps2[0:1, :], c.ones128b[:], tmp_b2[s][:], start=(fc == 0), stop=(fc == NFC - 1))
                        _inc(c, "pe", inst)

                    pe_fc[fc] = c.n["pe"]

                def _rowmath():
                    pe_stats = c.n["pe"]

                    @blk.scalar
                    def _(se):
                        _w(se, c, "pe", pe_stats)
                        se.copy(rows[:, SX:SX + TB], ps1[0:1, :])
                        _inc(c, "se", se.copy(rows[:, SX2:SX2 + TB], ps2[0:1, :]))

                    se_rows = c.n["se"]

                    @blk.vector
                    def _(ve):
                        _w(ve, c, "se", se_rows)
                        ve.tensor_scalar(rows[:, MU:MU + TB], rows[:, SX:SX + TB], INV_E, None, ALU.mult)
                        ve.tensor_scalar(rows[:, VAR:VAR + TB], rows[:, SX2:SX2 + TB], INV_E, None, ALU.mult)
                        ve.tensor_tensor(rows[:, RRW:RRW + TB], rows[:, MU:MU + TB], rows[:, MU:MU + TB], ALU.mult)
                        _inc(c, "ve", ve.tensor_tensor(rows[:, VAR:VAR + TB], rows[:, VAR:VAR + TB], rows[:, RRW:RRW + TB], ALU.subtract))

                    ve_var = c.n["ve"]

                    @blk.scalar
                    def _(se):
                        _w(se, c, "ve", ve_var)
                        se.activation(rows[:, RRW:RRW + TB], rows[:, VAR:VAR + TB], AF.Ln, bias=EPS)
                        _inc(c, "se", se.activation(rows[:, RRW:RRW + TB], rows[:, RRW:RRW + TB], AF.Exp, scale=-0.5))

                    _emit_bcast(c, rows[0:1, MU:MU + TB], mu_b, ps_bc, evac_eng="scalar")
                    _emit_bcast(c, rows[0:1, RRW:RRW + TB], r_b, ps_bc, evac_eng="scalar")
                    return c.n["se"]

                se_bc = _rowmath()

                @blk.vector
                def _(ve, se_bc=se_bc):
                    _w(ve, c, "se", se_bc)
                    for fc in range(NFC):
                        ve.tensor_tensor(Xb[:, fc * TB:(fc + 1) * TB], Xb[:, fc * TB:(fc + 1) * TB], mu_b[:], ALU.subtract)
                        inst = ve.tensor_tensor(a1[:, fc * TB:(fc + 1) * TB], Xb[:, fc * TB:(fc + 1) * TB], r_b[:], ALU.mult)
                    _inc(c, "ve", inst)

                ve_a1 = c.n["ve"]

                pe_fc = {}
                for fc in range(NFC):
                    s = fc % 2

                    @blk.scalar
                    def _(se, fc=fc, s=s, ve_a1=ve_a1):
                        if fc == 0:
                            _w(se, c, "ve", ve_a1)
                            _w(se, c, "pe", c.n["pe"])
                        if fc >= 2:
                            _w(se, c, "pe", pe_fc[fc - 2])
                        _inc(c, "se", se.activation(tmp_b2[s][:], a1[:, fc * TB:(fc + 1) * TB], AF.Square))

                    se_now = c.n["se"]

                    @blk.tensor
                    def _(te, fc=fc, s=s, se_now=se_now):
                        _w(te, c, "se", se_now)
                        te.matmul(ps1[0:1, :], c.ones128b[:], a1[:, fc * TB:(fc + 1) * TB], start=(fc == 0), stop=(fc == NFC - 1))
                        inst = te.matmul(ps2[0:1, :], c.ones128b[:], tmp_b2[s][:], start=(fc == 0), stop=(fc == NFC - 1))
                        _inc(c, "pe", inst)

                    pe_fc[fc] = c.n["pe"]

                se_bc2 = _rowmath()

                @blk.vector
                def _(ve, se_bc2=se_bc2):
                    _w(ve, c, "se", se_bc2)
                    for fc in range(NFC):
                        ve.tensor_tensor(tmp_b[0][:], a1[:, fc * TB:(fc + 1) * TB], mu_b[:], ALU.subtract)
                        inst = ve.tensor_tensor(aa[:, fc * TB:(fc + 1) * TB], tmp_b[0][:], r_b[:], ALU.mult)
                    _inc(c, "ve", inst)

                ve_a = c.n["ve"]

                @blk.gpsimd
                def _(gp, tb=tb, ve_a=ve_a):
                    _w(gp, c, "ve", ve_a)
                    for tt in range(TB // 128):
                        src = aa[:].rearrange("p (cc t) -> p cc t", cc=NFC)[:, :, tt * 128:(tt + 1) * 128]
                        dst = c.a_scr[(tb * (TB // 128) + tt) * E:(tb * (TB // 128) + tt + 1) * E, :].rearrange("(cc p) t -> p cc t", p=128)
                        _inc(c, "gp", gp.dma_start(out=dst, in_=src), 16)

                @blk.tensor
                def _(te, ve_a=ve_a):
                    _w(te, c, "ve", ve_a)
                    _w(te, c, "se", c.n["se"])
                    for fc in range(NFC):
                        inst = te.matmul(ps1[:], c.wkc_g[:, fc * 128:(fc + 1) * 128].rearrange("p n -> n p") if False else c.wkc_g[fc], aa[:, fc * TB:(fc + 1) * TB], start=(fc == 0), stop=(fc == NFC - 1))
                    _inc(c, "pe", inst)

                pe_kc = c.n["pe"]

                @blk.scalar
                def _(se, pe_kc=pe_kc):
                    _w(se, c, "pe", pe_kc)
                    _inc(c, "se", se.copy(kc_sb[:], ps1[:]))

                se_kc = c.n["se"]

                @blk.tensor
                def _(te, se_kc=se_kc):
                    _w(te, c, "se", se_kc)
                    for fc in range(NFC):
                        inst = te.matmul(ps2[0:64, :], c.wqa_g[fc], aa[:, fc * TB:(fc + 1) * TB], start=(fc == 0), stop=(fc == NFC - 1))
                    _inc(c, "pe", inst)

                pe_aq = c.n["pe"]

                @blk.scalar
                def _(se, pe_aq=pe_aq):
                    _w(se, c, "pe", pe_aq)
                    _inc(c, "se", se.copy(aq1[:], ps2[0:64, :]))

                se_aq = c.n["se"]

                for h in range(H):
                    for which in range(2):
                        pslot = (h * 2 + which) % 2

                        @blk.tensor
                        def _(te, h=h, which=which, pslot=pslot, se_aq=se_aq):
                            _w(te, c, "se", max(se_aq, c.psfree_s.get(("p1p", pslot), 0)))
                            _w(te, c, "ve", c.psfree.get(("p1p", pslot), 0))
                            if which == 0:
                                _inc(c, "pe", te.matmul(ps_p[pslot][:], c.wqb_sb[:, h * 128:(h + 1) * 128], aq1[:], start=True, stop=True))
                            else:
                                _inc(c, "pe", te.matmul(ps_p[pslot][:], c.wkvb_sb[:, h * 128:(h + 1) * 128], kc_sb[:], start=True, stop=True))

                        pe_p = c.n["pe"]

                        @blk.scalar
                        def _(se, pslot=pslot, pe_p=pe_p):
                            _w(se, c, "pe", pe_p)
                            _inc(c, "se", se.copy(qk_sb[pslot][:], ps_p[pslot][:]))

                        se_p = c.n["se"]
                        c.psfree_s[("p1p", pslot)] = se_p

                        @blk.tensor
                        def _(te, pslot=pslot, se_p=se_p):
                            _w(te, c, "se", se_p)
                            _w(te, c, "ve", c.psfree.get("p1B", 0))
                            _inc(c, "pe", te.matmul(ps_B[0:64, :], c.JT_sb[:], qk_sb[pslot][:], start=True, stop=True))

                        pe_B = c.n["pe"]

                        @blk.vector
                        def _(ve, pslot=pslot, pe_B=pe_B, tb=tb):
                            _w(ve, c, "pe", pe_B)
                            _w(ve, c, "gp", c.qfree.get(pslot, 0))
                            ve.tensor_tensor(tcos[:], qk_sb[pslot][0:64, :], c.cosT_sb[:, tb * TB:(tb + 1) * TB], ALU.mult)
                            ve.tensor_tensor(tsin[:], ps_B[0:64, :], c.sinT_sb[:, tb * TB:(tb + 1) * TB], ALU.mult)
                            ve.tensor_tensor(qfin[pslot][0:64, :], tcos[:], tsin[:], ALU.add)
                            _inc(c, "ve", ve.tensor_copy(qfin[pslot][64:128, :], qk_sb[pslot][64:128, :]))

                        ve_f = c.n["ve"]
                        c.psfree[("p1p", pslot)] = ve_f
                        c.psfree["p1B"] = ve_f

                        if which == 0:
                            @blk.gpsimd
                            def _(gp, h=h, tb=tb, pslot=pslot, ve_f=ve_f):
                                _w(gp, c, "ve", ve_f)
                                _inc(c, "gp", gp.dma_start(out=c.q_scr[h * 128:(h + 1) * 128, tb * TB:(tb + 1) * TB], in_=qfin[pslot][:]), 16)

                            c.qfree[pslot] = c.n["gp"]
                        else:
                            @blk.vector
                            def _(ve, h=h, tb=tb, pslot=pslot):
                                _inc(c, "ve", ve.tensor_copy(K_res[:, h * T + tb * TB: h * T + (tb + 1) * TB], qfin[pslot][:]))
            _join(c)

        V_res = en(nc.sbuf_tensor("at_V", [128, (T // 128) * E], BF16))
        with contextlib.ExitStack() as s2:
            e2 = s2.enter_context
            wv = e2(nc.sbuf_tensor("p1b_wv", [128, (NFC // 2) * TB], F32))
            wvb = e2(nc.sbuf_tensor("p1b_wvb", [128, NFC * TB], BF16))
            asl = [e2(nc.sbuf_tensor(f"p1b_a_{i}", [128, NFC * 128], BF16)) for i in range(2)]
            ps_v = [e2(nc.psum_tensor(f"p1b_psv_{i}", [128, TB], F32)) for i in range(2)]

            for vc in range(4):
                for wh in range(2):
                    @blk.sync
                    def _(sy, vc=vc, wh=wh):
                        _w(sy, c, "ve", c.psfree.get("p1b_w", 0))
                        src = c.wkv_v_f32[wh * 1024:(wh + 1) * 1024, vc * TB:(vc + 1) * TB].rearrange("(cc p) n -> p cc n", p=128)
                        _inc(c, "ld", sy.dma_start(out=wv[:].rearrange("p (cc n) -> p cc n", cc=NFC // 2), in_=src), 16)

                    ld_wv = c.n["ld"]

                    @blk.vector
                    def _(ve, wh=wh, ld_wv=ld_wv):
                        _w(ve, c, "ld", ld_wv)
                        _w(ve, c, "pe", c.psfree_pe.get("p1b_mm", 0))
                        for fc in range(NFC // 2):
                            gfc = wh * (NFC // 2) + fc
                            inst = ve.tensor_scalar(wvb[:, gfc * TB:(gfc + 1) * TB], wv[:, fc * TB:(fc + 1) * TB], c.gkv_sb[:, gfc:gfc + 1], None, ALU.mult)
                        _inc(c, "ve", inst)

                    c.psfree["p1b_w"] = c.n["ve"]

                ve_wv = c.n["ve"]

                for tt in range(T // 128):
                    aslot = tt % 2

                    @blk.sync
                    def _(sy, tt=tt, aslot=aslot):
                        _w(sy, c, "pe", c.psfree.get(("p1b_a", aslot), 0))
                        _inc(c, "ld", sy.dma_start(out=asl[aslot][:].rearrange("p (cc t) -> p cc t", cc=NFC),
                                                   in_=c.a_scr[tt * E:(tt + 1) * E, :].rearrange("(cc p) t -> p cc t", p=128)), 16)

                    ld_a = c.n["ld"]
                    pslot = tt % 2

                    @blk.tensor
                    def _(te, tt=tt, aslot=aslot, pslot=pslot, ld_a=ld_a, ve_wv=ve_wv):
                        _w(te, c, "ld", ld_a)
                        _w(te, c, "ve", ve_wv)
                        _w(te, c, "se", c.psfree_s.get(("p1b", pslot), 0))
                        for fc in range(NFC):
                            inst = te.matmul(ps_v[pslot][:], asl[aslot][:, fc * 128:(fc + 1) * 128],
                                             wvb[:, fc * TB:(fc + 1) * TB], start=(fc == 0), stop=(fc == NFC - 1))
                        _inc(c, "pe", inst)

                    pe_v = c.n["pe"]
                    c.psfree[("p1b_a", aslot)] = pe_v
                    c.psfree_pe["p1b_mm"] = pe_v

                    @blk.scalar
                    def _(se, tt=tt, vc=vc, pslot=pslot, pe_v=pe_v):
                        _w(se, c, "pe", pe_v)
                        _inc(c, "se", se.copy(V_res[:, tt * E + vc * TB: tt * E + (vc + 1) * TB], ps_v[pslot][:]))

                    c.psfree_s[("p1b", pslot)] = c.n["se"]
            _join(c)

        with contextlib.ExitStack() as s3:
            e3 = s3.enter_context
            qsl = e3(nc.sbuf_tensor("p2_q", [128, TB], BF16))
            pT = [e3(nc.sbuf_tensor(f"p2_pT_{i}", [128, TB], BF16)) for i in range(3)]
            yT = e3(nc.sbuf_tensor("p2_yT", [128, NFC * TB], BF16))
            den_sb = e3(nc.sbuf_tensor("p2_den", [16, TB], F32))
            den_flat = e3(nc.sbuf_tensor("p2_denf", [1, 2 * TB], F32))
            rd_b = [e3(nc.sbuf_tensor(f"p2_rdb_{i}", [128, TB], BF16)) for i in range(2)]
            wo_sb = [e3(nc.sbuf_tensor(f"p2_wo_{i}", [128, NFC * 128], BF16)) for i in range(2)]
            xchunk = [e3(nc.sbuf_tensor(f"p2_xc_{i}", [128, TB], F32)) for i in range(2)]
            xout = [e3(nc.sbuf_tensor(f"p2_xo_{i}", [128, TB], F32)) for i in range(2)]
            ps_s = [e3(nc.psum_tensor(f"p2_pss_{i}", [128, TB], F32)) for i in range(2)]
            ps_y = [e3(nc.psum_tensor(f"p2_psy_{i}", [128, TB], F32)) for i in range(2)]
            pden = e3(nc.psum_tensor("p2_pden", [16, TB], F32))
            ps_bc = e3(nc.psum_tensor("p2_psbc", [128, TB], F32))

            wo_evts = [0] * 2
            for qb in range(NTB):
                nkt = (qb + 1) * (TB // 128)
                for h in range(H):
                    @blk.sync
                    def _(sy, qb=qb, h=h):
                        _w(sy, c, "pe", c.psfree.get("p2_q", 0))
                        _inc(c, "ld", sy.dma_start(out=qsl[:], in_=c.q_scr[h * 128:(h + 1) * 128, qb * TB:(qb + 1) * TB]), 16)

                    ld_q = c.n["ld"]
                    yslot = h % 2

                    for kt in range(nkt):
                        sslot = kt % 2
                        pslot = kt % 3
                        diag_m = kt - qb * (TB // 128)
                        col0 = max(0, diag_m * 128)

                        @blk.tensor
                        def _(te, qb=qb, h=h, kt=kt, sslot=sslot, col0=col0, ld_q=ld_q):
                            _w(te, c, "ld", ld_q)
                            _w(te, c, "se", c.psfree_s.get(("p2s", sslot), 0))
                            _inc(c, "pe", te.matmul(ps_s[sslot][:, col0:TB],
                                                    K_res[:, h * T + kt * 128: h * T + (kt + 1) * 128],
                                                    qsl[:, col0:TB], start=True, stop=True))

                        pe_s = c.n["pe"]

                        @blk.scalar
                        def _(se, sslot=sslot, pslot=pslot, col0=col0, pe_s=pe_s):
                            _w(se, c, "pe", pe_s)
                            _w(se, c, "pe", c.psfree_pe.get(("p2p", pslot), 0))
                            _w(se, c, "ve", c.psfree.get(("p2p", pslot), 0))
                            _inc(c, "se", se.activation(pT[pslot][:, col0:TB], ps_s[sslot][:, col0:TB], AF.Exp,
                                                        bias=MBIAS, scale=SM_SCALE))

                        se_p = c.n["se"]
                        c.psfree_s[("p2s", sslot)] = se_p

                        if diag_m >= 0:
                            @blk.vector
                            def _(ve, pslot=pslot, col0=col0, se_p=se_p):
                                _w(ve, c, "se", se_p)
                                _inc(c, "ve", ve.tensor_tensor(pT[pslot][:, col0:col0 + 128], pT[pslot][:, col0:col0 + 128], c.tri_sb[:], ALU.mult))

                        ve_p = c.n["ve"]

                        @blk.tensor
                        def _(te, qb=qb, h=h, kt=kt, pslot=pslot, yslot=yslot, col0=col0, se_p=se_p, ve_p=ve_p, diag_m=diag_m, nkt=nkt):
                            if diag_m >= 0:
                                _w(te, c, "ve", ve_p)
                            else:
                                _w(te, c, "se", se_p)
                            if kt == 0:
                                _w(te, c, "se", c.psfree_s.get(("p2y", yslot), 0))
                            te.matmul(ps_y[yslot][:, col0:TB],
                                      V_res[:, kt * E + h * 128: kt * E + (h + 1) * 128],
                                      pT[pslot][:, col0:TB], start=(kt == 0), stop=(kt == nkt - 1))
                            inst = te.matmul(pden[:, col0:TB], c.ind_sb[:, h * 16:(h + 1) * 16],
                                             pT[pslot][:, col0:TB], start=(h == 0 and kt == 0),
                                             stop=(h == H - 1 and kt == nkt - 1))
                            _inc(c, "pe", inst)

                        c.psfree_pe[("p2p", pslot)] = c.n["pe"]
                        c.psfree[("p2p", pslot)] = c.n["ve"]

                    pe_avh = c.n["pe"]

                    @blk.scalar
                    def _(se, h=h, yslot=yslot, pe_avh=pe_avh):
                        _w(se, c, "pe", pe_avh)
                        _inc(c, "se", se.copy(yT[:, h * TB:(h + 1) * TB], ps_y[yslot][:]))

                    c.psfree["p2_q"] = c.n["pe"]
                    c.psfree_s[("p2y", yslot)] = c.n["se"]
                pe_all = c.n["pe"]

                @blk.scalar
                def _(se, pe_all=pe_all):
                    _w(se, c, "pe", pe_all)
                    se.copy(den_sb[:], pden[:])
                    se.activation(den_sb[:], den_sb[:], AF.Ln)
                    _inc(c, "se", se.activation(den_sb[:], den_sb[:], AF.Exp, scale=-1.0))

                se_den = c.n["se"]

                for h in range(H):
                    slot = h % 2

                    @blk.gpsimd
                    def _(gp, h=h, slot=slot, se_den=se_den):
                        _w(gp, c, "se", se_den)
                        _w(gp, c, "pe", c.psfree_pe.get(("p2den", slot), 0))
                        _inc(c, "gp", gp.dma_start(out=den_flat[0:1, slot * TB:(slot + 1) * TB], in_=den_sb[h:h + 1, :]), 16)

                    gp_den = c.n["gp"]
                    _emit_bcast(c, den_flat[0:1, slot * TB:(slot + 1) * TB], rd_b[slot], ps_bc,
                                evac_eng="scalar", extra=("gp", gp_den))
                    c.psfree_pe[("p2den", slot)] = c.n["pe"]
                    se_rd = c.n["se"]

                    @blk.vector
                    def _(ve, h=h, slot=slot, se_rd=se_rd):
                        _w(ve, c, "se", se_rd)
                        _inc(c, "ve", ve.tensor_tensor(yT[:, h * TB:(h + 1) * TB], yT[:, h * TB:(h + 1) * TB], rd_b[slot][:], ALU.mult))

                ve_yT = c.n["ve"]

                for eo in range(NFC):
                    ws = eo % 2

                    @blk.gpsimd
                    def _(gp, eo=eo, ws=ws):
                        _w(gp, c, "pe", wo_evts[ws])
                        src = c.wo_f32[:, eo * 128:(eo + 1) * 128].rearrange("(cc p) n -> p cc n", p=128)
                        _inc(c, "gp", gp.dma_start(out=wo_sb[ws][:].rearrange("p (cc n) -> p cc n", cc=NFC), in_=src), 16)

                    gp_wo = c.n["gp"]

                    @blk.sync
                    def _(sy, eo=eo, qb=qb):
                        _w(sy, c, "ve", c.xcfree.get(("p2", eo % 2), 0))
                        _inc(c, "ld", sy.dma_start(out=xchunk[eo % 2][:], in_=x_io[eo * 128:(eo + 1) * 128, qb * TB:(qb + 1) * TB]), 16)

                    ld_xc = c.n["ld"]
                    pslot = eo % 2

                    @blk.tensor
                    def _(te, eo=eo, ws=ws, pslot=pslot, gp_wo=gp_wo, ve_yT=ve_yT):
                        _w(te, c, "gp", gp_wo)
                        _w(te, c, "ve", max(ve_yT, c.psfree.get(("p2wo", pslot), 0)))
                        for ey in range(NFC):
                            inst = te.matmul(ps_s[pslot][:], wo_sb[ws][:, ey * 128:(ey + 1) * 128],
                                             yT[:, ey * TB:(ey + 1) * TB], start=(ey == 0), stop=(ey == NFC - 1))
                        _inc(c, "pe", inst)

                    pe_wo = c.n["pe"]
                    wo_evts[ws] = pe_wo
                    c.psfree_s[("p2s", pslot)] = 0

                    @blk.vector
                    def _(ve, eo=eo, pslot=pslot, pe_wo=pe_wo, ld_xc=ld_xc):
                        _w(ve, c, "pe", pe_wo)
                        _w(ve, c, "ld", ld_xc)
                        _inc(c, "ve", ve.scalar_tensor_tensor(xout[eo % 2][:], ps_s[pslot][:], 1.0, xchunk[eo % 2][:], ALU.mult, ALU.add))

                    ve_xo = c.n["ve"]
                    c.psfree[("p2wo", pslot)] = ve_xo
                    c.xcfree[("p2", eo % 2)] = ve_xo

                    @blk.sync
                    def _(sy, eo=eo, qb=qb, ve_xo=ve_xo):
                        _w(sy, c, "ve", ve_xo)
                        _inc(c, "ld", sy.dma_start(out=x_io[eo * 128:(eo + 1) * 128, qb * TB:(qb + 1) * TB], in_=xout[eo % 2][:]), 16)

                _join(c)
            _join(c)


def build_nc(stages=("ffn1", "attn", "ffn2"), dbg=False):
    nc = bass.Bass("TRN2", target_bir_lowering=False, detect_race_conditions=False)
    c = _Ctx()
    c.psfree = {}
    c.psfree_s = {}
    c.psfree_pe = {}
    c.hfree = {}
    c.xcfree = {}
    c.qfree = {}

    x_in = nc.dram_tensor("x_fm", [E, T], F32, kind="ExternalInput").ap()
    out = nc.dram_tensor("out_fm", [E, T], F32, kind="ExternalOutput").ap()
    c.w1T_f32 = nc.dram_tensor("w1T", [E, II * 2], F32, kind="ExternalInput").ap()
    c.w2T_f32 = nc.dram_tensor("w2T", [II, E], F32, kind="ExternalInput").ap()
    c.wo_f32 = nc.dram_tensor("wo", [E, E], F32, kind="ExternalInput").ap()
    c.wkv_v_f32 = nc.dram_tensor("wkv_v", [E, E], F32, kind="ExternalInput").ap()
    wqa_d = nc.dram_tensor("wqa", [E, QR], F32, kind="ExternalInput").ap()
    wqb_d = nc.dram_tensor("wqb", [QR, E], F32, kind="ExternalInput").ap()
    wkc_d = nc.dram_tensor("wkc", [E, 128], F32, kind="ExternalInput").ap()
    wkvb_d = nc.dram_tensor("wkvb", [128, E], F32, kind="ExternalInput").ap()
    gq_d = nc.dram_tensor("gq", [128, NFC], F32, kind="ExternalInput").ap()
    gkv_d = nc.dram_tensor("gkv", [128, NFC], F32, kind="ExternalInput").ap()
    identb_d = nc.dram_tensor("identb", [128, 128], BF16, kind="ExternalInput").ap()
    jt_d = nc.dram_tensor("jt", [128, 64], BF16, kind="ExternalInput").ap()
    tri_d = nc.dram_tensor("tri", [128, 128], BF16, kind="ExternalInput").ap()
    ind_d = nc.dram_tensor("ind", [128, H * 16], BF16, kind="ExternalInput").ap()
    cos_d = nc.dram_tensor("cosT", [64, T], BF16, kind="ExternalInput").ap()
    sin_d = nc.dram_tensor("sinT", [64, T], BF16, kind="ExternalInput").ap()
    ones_d = nc.dram_tensor("onesrows", [1, 5 * 128], F32, kind="ExternalInput").ap()
    ones128_d = nc.dram_tensor("ones128b", [128, 1], BF16, kind="ExternalInput").ap()
    cst_d = nc.dram_tensor("consts", [128, 4], F32, kind="ExternalInput").ap()

    dbg_kind = "ExternalOutput" if dbg else "Internal"
    c.h_scr = nc.dram_tensor("h_scr", [II, T], BF16, kind=dbg_kind).ap()
    c.q_scr = nc.dram_tensor("q_scr", [E, T], BF16, kind=dbg_kind).ap()
    c.a_scr = nc.dram_tensor("a_scr", [(T // 128) * E, 128], BF16, kind=dbg_kind).ap()
    xwork = nc.dram_tensor("x_work", [E, T], F32, kind=dbg_kind).ap()
    if dbg:
        c.dbg = {
            "xq": nc.dram_tensor("dbg_xq", [128, NFC * T], BF16, kind="ExternalOutput").ap(),
            "hsrows": nc.dram_tensor("dbg_hsrows", [2, T], F32, kind="ExternalOutput").ap(),
            "rows": nc.dram_tensor("dbg_rows", [1, 8 * TB], F32, kind="ExternalOutput").ap(),
            "mub": nc.dram_tensor("dbg_mub", [128, TB], F32, kind="ExternalOutput").ap(),
            "rb": nc.dram_tensor("dbg_rb", [128, TB], F32, kind="ExternalOutput").ap(),
            "rsb": nc.dram_tensor("dbg_rsb", [128, TB], BF16, kind="ExternalOutput").ap(),
            "sb": nc.dram_tensor("dbg_sb", [128, TB], BF16, kind="ExternalOutput").ap(),
            "xn": nc.dram_tensor("dbg_xn", [128, NFC * TB], BF16, kind="ExternalOutput").ap(),
            "amb": nc.dram_tensor("dbg_amb", [128, TB], BF16, kind="ExternalOutput").ap(),
            "rows0": nc.dram_tensor("dbg_rows0", [1, 8 * TB], F32, kind="ExternalOutput").ap(),
            "rsb0": nc.dram_tensor("dbg_rsb0", [128, TB], BF16, kind="ExternalOutput").ap(),
            "sb0": nc.dram_tensor("dbg_sb0", [128, TB], BF16, kind="ExternalOutput").ap(),
            "amb0": nc.dram_tensor("dbg_amb0", [128, TB], BF16, kind="ExternalOutput").ap(),
            "comp0": nc.dram_tensor("dbg_comp0", [128, 4], F32, kind="ExternalOutput").ap(),
            "cbf0": nc.dram_tensor("dbg_cbf0", [128, 4], BF16, kind="ExternalOutput").ap(),
            "sgrid0": nc.dram_tensor("dbg_sgrid0", [4, 128], F32, kind="ExternalOutput").ap(),
        }
    else:
        c.dbg = None

    with contextlib.ExitStack() as stack:
        en = stack.enter_context
        _mk_sems(c, nc, stack)
        c.identb = en(nc.sbuf_tensor("g_identb", [128, 128], BF16))
        c.JT_sb = en(nc.sbuf_tensor("g_jt", [128, 64], BF16))
        c.tri_sb = en(nc.sbuf_tensor("g_tri", [128, 128], BF16))
        c.ind_sb = en(nc.sbuf_tensor("g_ind", [128, H * 16], BF16))
        c.cosT_sb = en(nc.sbuf_tensor("g_cos", [64, T], BF16))
        c.sinT_sb = en(nc.sbuf_tensor("g_sin", [64, T], BF16))
        onesrows = en(nc.sbuf_tensor("g_ones", [1, 5 * 128], F32))
        c.ones1 = onesrows[:, 0:128]
        c.onesn = onesrows[:, 128:256]
        c.onesinv = onesrows[:, 256:384]
        c.oneshalfinv = onesrows[:, 384:512]
        c.ones127 = onesrows[:, 512:640]
        c.ones128b = en(nc.sbuf_tensor("g_ones128", [128, 1], BF16))
        c.gq_sb = en(nc.sbuf_tensor("g_gq", [128, NFC], F32))
        c.gkv_sb = en(nc.sbuf_tensor("g_gkv", [128, NFC], F32))
        wqa_g_t = en(nc.sbuf_tensor("g_wqag", [128, NFC * QR], BF16))
        wkc_g_t = en(nc.sbuf_tensor("g_wkcg", [128, NFC * 128], BF16))
        c.wqa_g = [wqa_g_t[:, fc * QR:(fc + 1) * QR] for fc in range(NFC)]
        c.wkc_g = [wkc_g_t[:, fc * 128:(fc + 1) * 128] for fc in range(NFC)]
        c.wqb_sb = en(nc.sbuf_tensor("g_wqb", [64, E], BF16))
        c.wkvb_sb = en(nc.sbuf_tensor("g_wkvb", [128, E], BF16))
        cst_sb = en(nc.sbuf_tensor("g_cst", [128, 4], F32))
        st_pro = stack.enter_context(contextlib.ExitStack())
        wqa_f = st_pro.enter_context(nc.sbuf_tensor("g_wqaf", [128, NFC * QR], F32))
        wkc_f = st_pro.enter_context(nc.sbuf_tensor("g_wkcf", [128, NFC * 128], F32))
        nc.const_aps.register(F32, EPS, cst_sb[:, 0:1])
        nc.const_aps.register(F32, MBIAS, cst_sb[:, 1:2])
        nc.const_aps.register(F32, SM_SCALE, cst_sb[:, 2:3])
        nc.const_aps.register(F32, 127e-8, cst_sb[:, 3:4])
        c.block = en(nc.Block())
        blk = c.block

        @blk.sync
        def _(sy):
            for dst, src in ((c.identb, identb_d), (c.JT_sb, jt_d), (c.tri_sb, tri_d),
                             (c.ind_sb, ind_d), (c.cosT_sb, cos_d), (c.sinT_sb, sin_d),
                             (onesrows, ones_d), (c.ones128b, ones128_d),
                             (cst_sb, cst_d), (c.gq_sb, gq_d), (c.gkv_sb, gkv_d)):
                _inc(c, "ld", sy.dma_start(out=dst[:], in_=src), 16)
            _inc(c, "ld", sy.dma_start(out=wqa_f[:].rearrange("p (cc n) -> p cc n", cc=NFC),
                                       in_=wqa_d.rearrange("(cc p) n -> p cc n", p=128)), 16)
            _inc(c, "ld", sy.dma_start(out=wkc_f[:].rearrange("p (cc n) -> p cc n", cc=NFC),
                                       in_=wkc_d.rearrange("(cc p) n -> p cc n", p=128)), 16)

        @blk.gpsimd
        def _(gp):
            _inc(c, "gp", gp.dma_start(out=c.wqb_sb[:], in_=wqb_d), 16)
            _inc(c, "gp", gp.dma_start(out=c.wkvb_sb[:], in_=wkvb_d), 16)

        ld_pro = c.n["ld"]

        @blk.vector
        def _(ve):
            _w(ve, c, "ld", ld_pro)
            for fc in range(NFC):
                ve.tensor_scalar(wqa_g_t[:, fc * QR:(fc + 1) * QR], wqa_f[:, fc * QR:(fc + 1) * QR], c.gq_sb[:, fc:fc + 1], None, ALU.mult)
                inst = ve.tensor_scalar(wkc_g_t[:, fc * 128:(fc + 1) * 128], wkc_f[:, fc * 128:(fc + 1) * 128], c.gkv_sb[:, fc:fc + 1], None, ALU.mult)
            _inc(c, "ve", inst)

        _join(c)
        st_pro.close()

        last = {"ffn1": "ffn1", "attn": "attn", "ffn2": "ffn2"}
        if "ffn1" in stages:
            dst1 = xwork if ("attn" in stages or "ffn2" in stages) else out
            _emit_ffn(c, nc, x_in, dst1, phase=0)
            cur = xwork if dst1 is xwork else out
        if "attn" in stages:
            _emit_attn(c, nc, xwork)
        if "ffn2" in stages:
            _emit_ffn(c, nc, xwork, out, phase=1)
        elif "attn" in stages:
            with contextlib.ExitStack() as sc:
                cp = sc.enter_context(nc.sbuf_tensor("cp_buf", [128, T], F32))
                for r in range(NFC):
                    @blk.sync
                    def _(sy, r=r):
                        _w(sy, c, "ld", c.n["ld"])
                        _inc(c, "ld", sy.dma_start(out=cp[:], in_=xwork[r * 128:(r + 1) * 128, :]), 16)
                        _w(sy, c, "ld", c.n["ld"])
                        _inc(c, "ld", sy.dma_start(out=out[r * 128:(r + 1) * 128, :], in_=cp[:]), 16)

        snap = dict(c.n)

        @blk.sync
        def _(sy):
            for k in ("ld", "gp", "pe", "ve", "se"):
                _w(sy, c, k, snap[k])

    return nc


def _host_prep(inputs):
    x = np.asarray(inputs["L0_attn_x"])
    w_qa = np.asarray(inputs["L0_attn_w_qa"])
    w_qb = np.asarray(inputs["L0_attn_w_qb"])
    w_kva = np.asarray(inputs["L0_attn_w_kva"])
    w_kvb = np.asarray(inputs["L0_attn_w_kvb"])
    w_o = np.asarray(inputs["L0_attn_w_o"])
    g_q = np.asarray(inputs["L0_attn_g_q"])
    g_kv = np.asarray(inputs["L0_attn_g_kv"])
    w_gu = np.asarray(inputs["L0_ffn_w_gate_up"])
    w_dn = np.asarray(inputs["L0_ffn_w_down"])

    idx = np.empty(2 * II, np.int64)
    for j in range(NIC):
        idx[j * 256:j * 256 + 128] = np.arange(j * 128, (j + 1) * 128)
        idx[j * 256 + 128:(j + 1) * 256] = np.arange(II + j * 128, II + (j + 1) * 128)
    w1T = np.ascontiguousarray(w_gu[idx, :].T)
    w2T = np.ascontiguousarray(w_dn.T)
    vcols = (128 + 256 * np.repeat(np.arange(H), D) + np.tile(np.arange(D), H))
    wkv_v = np.ascontiguousarray(w_kva[:, vcols])
    wkc = np.ascontiguousarray(w_kva[:, 0:128])

    identb = np.eye(128).astype(BF)
    jt = np.zeros((128, 64), BF)
    for i in range(32):
        jt[i + 32, i] = -1.0
        jt[i, i + 32] = 1.0
    tri = np.tril(np.ones((128, 128), np.float32)).astype(BF)
    tri = np.triu(np.ones((128, 128), np.float32)).astype(BF)
    ind = np.zeros((128, H * 16), BF)
    for h in range(H):
        ind[:, h * 16 + h] = 1.0
    inv = 1.0 / (10000.0 ** (np.arange(0, 64, 2, dtype=np.float64) / 64.0))
    freqs = np.arange(T, dtype=np.float64)[:, None] * inv[None, :]
    emb = np.concatenate([freqs, freqs], axis=1)
    cosT = np.cos(emb).T.astype(BF)
    sinT = np.sin(emb).T.astype(BF)
    onesrows = np.concatenate([np.ones(128, np.float32), np.full(128, INV_E, np.float32),
                               np.full(128, INV127, np.float32),
                               np.full(128, 0.5 * INV127, np.float32),
                               np.full(128, 127.0, np.float32)])[None, :]
    ones128b = np.ones((128, 1), BF)

    shared = dict(
        w1T=w1T, w2T=w2T, wo=np.ascontiguousarray(w_o), wkv_v=wkv_v,
        wqa=np.ascontiguousarray(w_qa), wqb=np.ascontiguousarray(w_qb),
        wkc=wkc, wkvb=np.ascontiguousarray(w_kvb),
        gq=np.ascontiguousarray(g_q.reshape(NFC, 128).T),
        gkv=np.ascontiguousarray(g_kv.reshape(NFC, 128).T),
        identb=identb, jt=jt, tri=tri, ind=ind, cosT=cosT, sinT=sinT,
        onesrows=onesrows, ones128b=ones128b,
        consts=np.tile(np.array([[EPS, MBIAS, SM_SCALE, 127e-8]], np.float32), (128, 1)),
    )
    per_core = [dict(shared, x_fm=np.ascontiguousarray(x[b].T)) for b in range(x.shape[0])]
    return per_core


def kernel(_stages=("ffn1", "attn", "ffn2"), **inputs):
    per_core = _host_prep(inputs)
    nc = build_nc(stages=_stages)
    res = run_bass_kernel_spmd(nc, per_core, core_ids=list(range(8)))
    outs = [np.ascontiguousarray(np.asarray(r["out_fm"]).T) for r in res.results]
    return np.stack(outs, axis=0)
